# version 02 (final)
# baseline (speedup 1.0000x reference)
import numpy as np
import ml_dtypes
import concourse.bass as bass
import concourse.mybir as mybir
from concourse.bass_utils import run_bass_kernel_spmd

F32 = mybir.dt.float32
BF16 = mybir.dt.bfloat16
AF = mybir.ActivationFunctionType
OP = mybir.AluOpType
AX = mybir.AxisListType.X

DUMP = []
P1_TILES = 16

MAGIC = 12582912.0
T, H, DH, S, R, FH, NT, ND = 2048, 2048, 2048, 16, 128, 4096, 16, 16
EPS = 1e-5
WIN = [(0, 511), (511, 1023), (1023, 1535), (1535, 2048)]
LMAX = 513
GROUPS = [[0, 1], [2, 3], [4, 5], [6, 7]]


class Prog:
    def __init__(self, nc):
        self.nc = nc
        self.q = {e: [] for e in ("sync", "tensor", "vector", "scalar", "gpsimd")}
        self.sems = {}
        for s in ("d", "s", "v", "t", "g", "c"):
            self.sems[s] = [None, 0]

    def cnt(self, s):
        return self.sems[s][1]

    def emit(self, engine, fn, waits=(), inc=None, amt=1):
        if inc is not None:
            self.sems[inc][1] += amt
        self.q[engine].append(([w for w in waits if w[1] > 0], fn, inc, amt))
        return self.sems[inc][1] if inc else None

    def dma(self, engine, dst, src, waits=(), sem=None):
        sem = sem or ("g" if engine == "gpsimd" else "d")
        return self.emit(engine, lambda e, d=dst, s=src: e.dma_start(out=d, in_=s),
                         waits=waits, inc=sem, amt=16)

    def barrier(self):
        snap = {s: self.sems[s][1] for s in self.sems}
        for e in self.q:
            for s, v in snap.items():
                if v > 0:
                    self.q[e].append(([(s, v)], None, None, 0))

    def run(self, engine, eng):
        for waits, fn, inc, amt in self.q[engine]:
            for (s, v) in waits:
                eng.wait_ge(self.sems[s][0], v)
            if fn is not None:
                inst = fn(eng)
                if inc is not None:
                    inst.then_inc(self.sems[inc][0], amt)




class _StopBuild(Exception):

    def __init__(self, p, nc, out_sl, big4):
        self.nc = nc
        p.barrier()
        for nm in DUMP:
            ap_, shp, dt_ = p.dbg[nm]
            dbg_o = nc.dram_tensor("dbg_" + nm, shp, dt_, kind="ExternalOutput").ap()
            p.dma("sync", dbg_o, ap_)
        for M in range(16):
            p.dma("sync", out_sl[M * 128:(M + 1) * 128, :], big4[:, 0:1024])
        p.q["sync"].append(([("d", p.cnt("d"))], None, None, 0))
        _finish(nc, p)

def build_program(stop_after=99):
    try:
        return _build(stop_after)
    except _StopBuild as e:
        return e.nc


def _build(stop_after=99):
    nc = bass.Bass("TRN2", target_bir_lowering=False)

    def din(name, shape, dt=F32):
        return nc.dram_tensor(name, shape, dt, kind="ExternalInput").ap()

    x_b = din("x_b", [T, H])
    x_res = din("x_res", [T, 1024])
    Win = din("Win", [H, 2 * DH])
    convw = din("convw", [128, ND * 3])
    convb = din("convb", [128, ND])
    Wx = din("Wx", [128, ND * 160])
    Wdt = din("Wdt", [R, DH])
    bdt = din("bdt", [128, ND])
    Alog = din("Alog", [128, ND * S])
    Dsl = din("Dsl", [128, ND])
    Wout = din("Wout", [DH, H])
    w1s = din("w1s", [FH, H])
    w2s = din("w2s", [H, FH])
    b1s = din("b1s", [1, FH])
    b2s = din("b2s", [1, 1024])
    ident = din("ident", [128, 128], BF16)
    identf = din("identf", [128, 128])
    ones_r = din("ones_r", [1, 128])
    ones_rb = din("ones_rb", [1, 128], BF16)
    ones_c = din("ones_c", [128, 1])
    out_sl = nc.dram_tensor("out_sl", [T, 1024], F32, kind="ExternalOutput").ap()

    def dsc(name, shape, dt=F32):
        return nc.dram_tensor(name, shape, dt).ap()

    d_WinB = dsc("d_WinB", [H, 2 * DH], BF16)
    d_WoutB = dsc("d_WoutB", [DH, H], BF16)
    d_w1qT = dsc("d_w1qT", [H, FH], BF16)
    d_w2qT = dsc("d_w2qT", [FH, H], BF16)
    d_xnT = dsc("d_xnT", [H, T], BF16)
    d_k1T = dsc("d_k1T", [H, T], BF16)
    d_k2T = dsc("d_k2T", [FH, T], BF16)
    d_z = dsc("d_z", [DH, T])
    d_yzB = dsc("d_yzB", [DH, T], BF16)
    d_gh = dsc("d_gh", [T, FH])
    d_dbg = dsc("d_dbg", [T, H])
    d_dbg2 = dsc("d_dbg2", [T, 48])
    cc_sc_i = dsc("cc_sc_i", [1, 2]); cc_sc_o = dsc("cc_sc_o", [1, 2])
    cc_xd_i = dsc("cc_xd_i", [160, LMAX]); cc_xd_o = dsc("cc_xd_o", [160, LMAX])
    cc_st_i = dsc("cc_st_i", [128, 32]); cc_st_o = dsc("cc_st_o", [128, 32])
    cc_sm_i = dsc("cc_sm_i", [128, 32]); cc_sm_o = dsc("cc_sm_o", [128, 32])
    rs1_i = dsc("rs1_i", [2, T, 1024]); rs1_o = dsc("rs1_o", [T, 1024])
    rs2_i = dsc("rs2_i", [2, T, 1024]); rs2_o = dsc("rs2_o", [T, 1024])

    p = Prog(nc)
    p.dbg = {"d_xnT": (d_xnT, [H, T], BF16), "d_k1T": (d_k1T, [H, T], BF16),
             "d_z": (d_z, [DH, T], F32), "d_yzB": (d_yzB, [DH, T], BF16),
             "d_gh": (d_gh, [T, FH], F32), "rs1_o": (rs1_o, [T, 1024], F32),
             "rs2_o": (rs2_o, [T, 1024], F32), "d_w1qT": (d_w1qT, [H, FH], BF16),
             "d_k2T": (d_k2T, [FH, T], BF16), "d_dbg": (d_dbg, [T, H], F32), "rs1_i": (rs1_i, [2, T, 1024], F32), "d_dbg2": (d_dbg2, [T, 48], F32)}
    p.nc_ref = nc
    import contextlib
    with contextlib.ExitStack() as st:
        def sb(name, shape, dt=F32):
            return st.enter_context(nc.sbuf_tensor(name, shape, dt))

        big1 = sb("big1", [128, ND * LMAX])
        big2 = sb("big2", [128, ND * LMAX])
        big3 = sb("big3", [128, ND * LMAX])
        big4 = sb("big4", [128, ND * LMAX])
        wbuf = sb("wbuf", [128, 16448], BF16)
        bf2 = sb("bf2", [128, ND * LMAX + 256], BF16)
        xpb = sb("xpb", [128, 515])
        hstate = sb("hstate", [128, 256])
        cwc = sb("cwc", [128, ND * 3]); cbc = sb("cbc", [128, ND])
        wxc = sb("wxc", [128, ND * 160], BF16); wdtc = sb("wdtc", [128, DH], BF16)
        bdtc = sb("bdtc", [128, ND]); aneg = sb("aneg", [128, ND * S])
        dcol = sb("dcol", [128, ND]); idb = sb("idb", [128, 128], BF16)
        idf = sb("idf", [128, 128])
        onr = sb("onr", [1, 128]); onrb = sb("onrb", [1, 128], BF16)
        onc = sb("onc", [128, 1])
        colA = sb("colA", [128, 48]); colB = sb("colB", [128, 32])
        scl1 = sb("scl1", [128, NT]); scl2 = sb("scl2", [128, NT])
        st_sum = sb("st_sum", [128, NT]); st_ssq = sb("st_ssq", [128, NT])
        st_mx = sb("st_mx", [128, NT]); st_mnn = sb("st_mnn", [128, NT])
        q2c = sb("q2c", [128, NT]); nm2c = sb("nm2c", [128, NT])
        wsums = sb("wsums", [128, 48]); thr = sb("thr", [128, 8])
        xdl = sb("xdl", [128, LMAX]); xdlo = sb("xdlo", [32, LMAX])
        dtib = sb("dtib", [128, LMAX], BF16)
        hcb = sb("hcb", [128, 2 * LMAX], BF16)
        xptail = sb("xptail", [128, ND * 2])
        kcol = sb("kcol", [128, 2])
        ytail = sb("ytail", [128, 2 * ND])
        ps = [st.enter_context(nc.psum_tensor("ps%d" % i, [128, 512], F32))
              for i in range(8)]
        for s in p.sems:
            p.sems[s][0] = st.enter_context(nc.semaphore("sem_" + s))

        def b4(c0, L):
            return big4[:, c0:c0 + L]

        b1rep = big2[:, 0:FH]
        b2rep = big2[:, 4104:4104 + 1024]

        def r3(ap_, p128, a, arange, crange):
            return ap_.rearrange("(a p) c -> p a c", p=p128)[:, arange[0]:arange[1],
                                                            crange[0]:crange[1]]

        p.dma("sync", cwc[:], convw); p.dma("sync", cbc[:], convb)
        p.dma("sync", bdtc[:], bdt); p.dma("sync", dcol[:], Dsl)
        p.dma("sync", idb[:], ident); p.dma("sync", idf[:], identf)
        p.dma("sync", onr[:], ones_r)
        p.dma("sync", onrb[:], ones_rb); p.dma("sync", onc[:], ones_c)
        p.dma("sync", aneg[:], Alog)
        p.dma("sync", big1[:, 0:DH], Wdt)
        p.dma("sync", big1[:, DH:DH + ND * 160], Wx)
        d0 = p.cnt("d")
        p.emit("scalar", lambda e: e.activation(aneg[:], aneg[:], AF.Exp),
               waits=[("d", d0)], inc="s")
        sA = p.emit("vector", lambda e: e.tensor_scalar_mul(aneg[:], aneg[:], -1.0),
                    waits=[("s", p.cnt("s"))], inc="v")
        p.emit("scalar", lambda e: e.copy(wdtc[:], big1[:, 0:DH]), waits=[("d", d0)], inc="s")
        p.emit("scalar", lambda e: e.copy(wxc[:], big1[:, DH:DH + ND * 160]), inc="s")
        p.emit("vector", lambda e: e.memset(hstate[:], 0.0), inc="v")
        p.emit("vector", lambda e: e.memset(xpb[:, 513:515], 0.0), inc="v")
        p.emit("vector", lambda e: e.memset(xptail[:], 0.0), inc="v")
        p.emit("vector", lambda e: e.memset(big2[:], 0.0), inc="v")
        p.emit("vector", lambda e: e.memset(st_sum[:], 0.0), inc="v")
        p.emit("vector", lambda e: e.memset(st_ssq[:], 0.0), inc="v")
        p.emit("vector", lambda e: e.memset(st_mx[:], -3.0e38), inc="v")
        p.emit("vector", lambda e: e.memset(st_mnn[:], -3.0e38), inc="v")
        p.emit("vector", lambda e: e.memset(kcol[:, 0:1], EPS), inc="v")
        p.emit("vector", lambda e: e.memset(kcol[:, 1:2], -0.5), inc="v")
        v0 = p.cnt("v")
        p.dma("gpsimd", cc_xd_i[0:128, :], big2[:, 0:LMAX], waits=[("v", v0)])
        p.dma("gpsimd", cc_xd_i[128:160, :], big2[0:32, 0:LMAX])

        if stop_after < 1: raise _StopBuild(p, nc, out_sl, big4)
        s_abs = [0] * 48
        for i in range(48):
            if i < 32:
                src = w1s[i * 128:(i + 1) * 128, :]; w_ = 2048
            else:
                src = w2s[(i - 32) * 128:(i - 31) * 128, :]; w_ = 4096
            buf = big3[:, (i % 2) * 4104:(i % 2) * 4104 + w_]
            gw = [("s", s_abs[i - 2])] if i >= 2 else []
            p.dma("sync", buf, src, waits=gw)
            dw = p.cnt("d")
            s_abs[i] = p.emit(
                "scalar", lambda e, b=buf, i=i, w_=w_: e.activation(
                    big4[:, 0:w_], b, AF.Abs, accum_out=wsums[:, i:i + 1]),
                waits=[("d", dw)], inc="s")
        p.emit("tensor", lambda e: e.matmul(ps[0][0:1, 0:48], onc[:], wsums[:],
                                            start=True, stop=True),
               waits=[("s", s_abs[47]), ("v", v0)], inc="t")
        vcp = p.emit("vector", lambda e: e.tensor_copy(colA[0:1, 0:48], ps[0][0:1, 0:48]),
                     waits=[("t", p.cnt("t"))], inc="v")
        vr1 = p.emit("vector", lambda e: e.tensor_reduce(colB[0:1, 0:1], colA[0:1, 0:32], AX, OP.add),
                     waits=[("v", vcp)], inc="v")
        vred = p.emit("vector", lambda e: e.tensor_reduce(colB[0:1, 1:2], colA[0:1, 32:48], AX, OP.add),
                      waits=[("v", vr1)], inc="v")
        p.dma("gpsimd", cc_sc_i, colB[0:1, 0:2], waits=[("v", vred)])
        p.emit("gpsimd", lambda e: e.collective_compute(
            "AllReduce", OP.add, replica_groups=GROUPS,
            ins=[cc_sc_i.opt()], outs=[cc_sc_o.opt()]),
            waits=[("g", p.cnt("g"))], inc="c")
        p.dma("gpsimd", colB[0:1, 4:6], cc_sc_o, waits=[("c", p.cnt("c"))])
        g1 = p.cnt("g")
        vt1 = p.emit("vector", lambda e: e.tensor_scalar_mul(colB[0:1, 8:10], colB[0:1, 4:6], 1.0 / 16777216.0),
                     waits=[("g", g1)], inc="v")
        vt2 = p.emit("vector", lambda e: e.tensor_scalar_mul(colB[0:1, 10:12], colB[0:1, 8:10], 0.5),
                     waits=[("v", vt1)], inc="v")
        vth = p.emit("vector", lambda e: e.tensor_scalar_mul(colB[0:1, 12:14], colB[0:1, 10:12], -1.0),
                     waits=[("v", vt2)], inc="v")
        p.emit("tensor", lambda e: e.matmul(ps[1][:, 0:14], onr[:], colB[0:1, 0:14],
                                            start=True, stop=True),
               waits=[("v", vth)], inc="t")
        vcp2 = p.emit("vector", lambda e: e.tensor_copy(colA[:, 0:14], ps[1][:, 0:14]),
                      waits=[("t", p.cnt("t"))], inc="v")
        for (src_c, dst_c) in ((10, 0), (12, 1), (8, 2), (11, 3), (13, 4), (9, 5)):
            p.emit("vector", lambda e, a=src_c, b_=dst_c: e.tensor_copy(thr[:, b_:b_ + 1], colA[:, a:a + 1]),
                   waits=[("v", vcp2)], inc="v")
        vthr = p.cnt("v")

        ev_s = [0, 0]
        ev_d = [0, 0]
        last_v = [0, 0]
        t_tern = 0
        for (n_tiles, src, ddst, n_dm, tc, ntc) in (
                (32, w1s, d_w1qT, 16, 0, 1), (16, w2s, d_w2qT, 32, 3, 4)):
            for i in range(n_tiles):
                w_ = n_dm * 128
                buf = big3[:, (i % 2) * 4104:(i % 2) * 4104 + w_]
                p.dma("sync", buf, src[i * 128:(i + 1) * 128, :],
                      waits=[("v", last_v[i % 2]), ("s", s_abs[47])])
                dw = p.cnt("d")
                p.emit("vector", lambda e, b=buf, w_=w_, tc=tc: e.tensor_scalar(
                    big4[:, 0:w_], b, thr[:, tc:tc + 1], None, OP.is_gt),
                    waits=[("d", dw), ("v", vthr)], inc="v")
                last_v[i % 2] = p.emit("vector", lambda e, b=buf, w_=w_, ntc=ntc: e.tensor_scalar(
                    big4[:, 4104:4104 + w_], b, thr[:, ntc:ntc + 1], None, OP.is_lt), inc="v")
                vq = p.emit("vector", lambda e, w_=w_: e.tensor_tensor(
                    big1[:, 0:w_], big4[:, 0:w_], big4[:, 4104:4104 + w_], OP.subtract),
                    waits=[("t", t_tern)], inc="v")
                for jb in range(n_dm):
                    pt = ps[2 + (jb % 2)]
                    tw = p.emit("tensor", lambda e, jb=jb, pt=pt: e.transpose(
                        pt[:, 0:128], big1[:, jb * 128:(jb + 1) * 128], idf[:]),
                        waits=[("v", vq), ("s", ev_s[jb % 2])], inc="t")
                    t_tern = tw
                    ev_s[jb % 2] = p.emit("scalar", lambda e, pt=pt, jb=jb: e.copy(
                        bf2[:, 8208 + (jb % 2) * 128:8336 + (jb % 2) * 128], pt[:, 0:128]),
                        waits=[("t", tw), ("d", ev_d[jb % 2])], inc="s")
                    ev_d[jb % 2] = p.dma(
                        "sync", ddst[jb * 128:(jb + 1) * 128, i * 128:(i + 1) * 128],
                        bf2[:, 8208 + (jb % 2) * 128:8336 + (jb % 2) * 128],
                        waits=[("s", ev_s[jb % 2])])
        cast_s = [0, 0]; cast_d = [0, 0]
        for (n_tiles, src, ddst, w_) in ((16, Win, d_WinB, 4096), (16, Wout, d_WoutB, 2048)):
            for i in range(n_tiles):
                buf = big3[:, (i % 2) * 4104:(i % 2) * 4104 + w_]
                p.dma("sync", buf, src[i * 128:(i + 1) * 128, :],
                      waits=[("v", last_v[i % 2]), ("s", cast_s[i % 2])])
                dw = p.cnt("d")
                cast_s[i % 2] = p.emit("scalar", lambda e, b=buf, i=i, w_=w_: e.copy(
                    bf2[:, (i % 2) * 4104:(i % 2) * 4104 + w_], b),
                    waits=[("d", dw), ("d", cast_d[i % 2])], inc="s")
                cast_d[i % 2] = p.dma(
                    "sync", ddst[i * 128:(i + 1) * 128, :],
                    bf2[:, (i % 2) * 4104:(i % 2) * 4104 + w_],
                    waits=[("s", cast_s[i % 2])])
        p.barrier()

        if stop_after < 2: raise _StopBuild(p, nc, out_sl, big4)
        tk1_t = 0
        s_xn_hist = [0, 0]
        v_mu_hist = [0, 0]
        prev_vk1 = 0
        for ti in range(P1_TILES):
            xb = big3[:, (ti % 2) * 4104:(ti % 2) * 4104 + 2048]
            p.dma("sync", xb, x_b[ti * 128:(ti + 1) * 128, :],
                  waits=[("s", s_xn_hist[ti % 2]), ("v", v_mu_hist[ti % 2])])
            dw = p.cnt("d")
            vr = p.emit("vector", lambda e, b=xb: e.tensor_reduce(colA[:, 0:1], b, AX, OP.add),
                        waits=[("d", dw)], inc="v")
            v_mu_hist[ti % 2] = vmu = p.emit("vector", lambda e: e.tensor_scalar_mul(
                colA[:, 1:2], colA[:, 0:1], -1.0 / 2048.0), waits=[("v", vr)], inc="v")
            ssq = p.emit("scalar", lambda e, b=xb: e.activation(
                big4[:, 2048:4096], b, AF.Square, accum_out=colA[:, 2:3]),
                waits=[("d", dw), ("v", prev_vk1)], inc="s")
            vm2 = p.emit("vector", lambda e: e.tensor_mul(colA[:, 3:4], colA[:, 1:2], colA[:, 1:2]),
                         waits=[("s", ssq), ("v", vmu)], inc="v")
            vv1 = p.emit("vector", lambda e: e.tensor_scalar(colA[:, 4:5], colA[:, 2:3], 1.0 / 2048.0, None, OP.mult), inc="v")
            vvar = p.emit("vector", lambda e: e.tensor_sub(colA[:, 4:5], colA[:, 4:5], colA[:, 3:4]),
                          waits=[("v", vv1)], inc="v")
            p.emit("scalar", lambda e: e.activation(colA[:, 5:6], colA[:, 4:5], AF.Ln, bias=kcol[:, 0:1]),
                   waits=[("v", vvar)], inc="s")
            sr = p.emit("scalar", lambda e: e.activation(colA[:, 5:6], colA[:, 5:6], AF.Exp, scale=kcol[:, 1:2]), inc="s")
            vnmr = p.emit("vector", lambda e: e.tensor_mul(colA[:, 6:7], colA[:, 1:2], colA[:, 5:6]),
                          waits=[("s", sr)], inc="v")
            s_xn = p.emit("scalar", lambda e, b=xb: e.activation(
                big4[:, 0:2048], b, AF.Identity, scale=colA[:, 5:6], bias=colA[:, 6:7],
                accum_out=colA[:, 8:9]),
                waits=[("v", vnmr), ("t", tk1_t)], inc="s")
            s_xn_hist[ti % 2] = s_xn
            if DUMP:
                p.dma("sync", d_dbg[ti * 128:(ti + 1) * 128, 0:2048], big4[:, 0:2048],
                      waits=[("s", s_xn)])
            sxq = p.emit("scalar", lambda e: e.activation(
                big4[:, 2048:4096], big4[:, 0:2048], AF.Square, accum_out=colA[:, 9:10]), inc="s")
            vmx = p.emit("vector", lambda e: e.tensor_reduce(colA[:, 10:11], big4[:, 0:2048], AX, OP.max),
                         waits=[("s", s_xn)], inc="v")
            vmn = p.emit("vector", lambda e: e.tensor_reduce(colA[:, 11:12], big4[:, 0:2048], AX, OP.min), inc="v")
            vm2b = p.emit("vector", lambda e: e.tensor_scalar_mul(colA[:, 12:13], colA[:, 8:9], -1.0 / 2048.0),
                          waits=[("s", sxq)], inc="v")
            vm3 = p.emit("vector", lambda e: e.tensor_mul(colA[:, 13:14], colA[:, 12:13], colA[:, 12:13]),
                         waits=[("v", vm2b)], inc="v")
            vm4 = p.emit("vector", lambda e: e.tensor_scalar(colA[:, 14:15], colA[:, 9:10], 1.0 / 2048.0, None, OP.mult), inc="v")
            vv2 = p.emit("vector", lambda e: e.tensor_sub(colA[:, 14:15], colA[:, 14:15], colA[:, 13:14]),
                         waits=[("v", vm4)], inc="v")
            p.emit("scalar", lambda e: e.activation(colA[:, 15:16], colA[:, 14:15], AF.Ln, bias=kcol[:, 0:1]),
                   waits=[("v", vv2)], inc="s")
            sr2 = p.emit("scalar", lambda e: e.activation(colA[:, 15:16], colA[:, 15:16], AF.Exp, scale=kcol[:, 1:2]), inc="s")
            va1 = p.emit("vector", lambda e: e.tensor_scalar(colA[:, 16:17], colA[:, 10:11], colA[:, 12:13], None, OP.add),
                         waits=[("s", sr2), ("v", vmx)], inc="v")
            va2 = p.emit("vector", lambda e: e.tensor_scalar(colA[:, 17:18], colA[:, 11:12], colA[:, 12:13], -1.0, OP.add, OP.mult), inc="v")
            va3 = p.emit("vector", lambda e: e.tensor_max(colA[:, 16:17], colA[:, 16:17], colA[:, 17:18]),
                         waits=[("v", va2)], inc="v")
            va4 = p.emit("vector", lambda e: e.tensor_mul(colA[:, 16:17], colA[:, 16:17], colA[:, 15:16]),
                         waits=[("v", va3)], inc="v")
            va5 = p.emit("vector", lambda e: e.tensor_scalar_max(colA[:, 18:19], colA[:, 16:17], EPS),
                         waits=[("v", va4)], inc="v")
            va6 = p.emit("vector", lambda e: e.reciprocal(colA[:, 19:20], colA[:, 18:19]),
                         waits=[("v", va5)], inc="v")
            va7 = p.emit("vector", lambda e: e.tensor_scalar(colA[:, 19:20], colA[:, 19:20], 127.0, None, OP.mult),
                        waits=[("v", va6)], inc="v")
            va8 = p.emit("vector", lambda e: e.tensor_mul(colA[:, 19:20], colA[:, 19:20], colA[:, 15:16]),
                         waits=[("v", va7)], inc="v")
            vs1 = p.emit("vector", lambda e, ti=ti: e.tensor_mul(scl1[:, ti:ti + 1], colA[:, 18:19], thr[:, 2:3]), inc="v")
            p.emit("vector", lambda e, ti=ti: e.tensor_scalar_mul(scl1[:, ti:ti + 1], scl1[:, ti:ti + 1], 1.0 / 127.0),
                   waits=[("v", vs1)], inc="v")
            vq_ = p.emit("vector", lambda e: e.tensor_mul(colA[:, 20:21], colA[:, 12:13], colA[:, 19:20]),
                         waits=[("v", va8)], inc="v")
            if DUMP:
                p.dma("sync", d_dbg2[ti * 128:(ti + 1) * 128, 0:48], colA[:],
                      waits=[("v", vq_)])
            p.emit("vector", lambda e: e.tensor_scalar(
                big4[:, 2048:4096], big4[:, 0:2048], colA[:, 19:20], colA[:, 20:21], OP.mult, OP.add),
                waits=[("v", vq_)], inc="v")
            vk1 = p.emit("vector", lambda e: e.tensor_scalar(
                big4[:, 4104:6152], big4[:, 2048:4096], MAGIC, -MAGIC, OP.add, OP.add),
                waits=[("t", tk1_t)], inc="v")
            prev_vk1 = vk1
            for (off, ddst, wv) in ((0, d_xnT, [("s", s_xn)]), (4104, d_k1T, [("v", vk1)])):
                for jb in range(16):
                    pt = ps[4 + (jb % 2)]
                    tw = p.emit("tensor", lambda e, pt=pt, o=off, jb=jb: e.transpose(
                        pt[:, 0:128], big4[:, o + jb * 128:o + jb * 128 + 128], idf[:]),
                        waits=wv + [("s", ev_s[jb % 2])], inc="t")
                    if off == 4104 and jb == 15:
                        tk1_t = tw
                    ev_s[jb % 2] = p.emit("scalar", lambda e, pt=pt, jb=jb: e.copy(
                        bf2[:, 8208 + (jb % 2) * 128:8336 + (jb % 2) * 128], pt[:, 0:128]),
                        waits=[("t", tw), ("d", ev_d[jb % 2])], inc="s")
                    ev_d[jb % 2] = p.dma(
                        "sync", ddst[jb * 128:(jb + 1) * 128, ti * 128:(ti + 1) * 128],
                        bf2[:, 8208 + (jb % 2) * 128:8336 + (jb % 2) * 128],
                        waits=[("s", ev_s[jb % 2])])
        p.barrier()

        if stop_after < 3: raise _StopBuild(p, nc, out_sl, big4)
        B4_DA = (0, 513); B4_X = 1026; B4_H = 1539; B4_Z = 2052
        B4_Y1 = 2565; B4_YZ = 3078; B4_SP1 = 3591; B4_SP2 = 4104
        B4_EV = (4617, 5130)
        XNS = 8192

        for ci in range(4):
            w0, w1e = WIN[ci]
            L = w1e - w0
            Lm = min(L, 512)
            for k4 in range(4):
                p.dma("sync",
                      wbuf[:, XNS + k4 * 2048:XNS + (k4 + 1) * 2048].rearrange(
                          "p (a c) -> p a c", a=4),
                      r3(d_xnT, 128, 4, (k4 * 4, k4 * 4 + 4), (ci * 512, ci * 512 + 512)))
            d_xn = p.cnt("d")
            if stop_after == 30 and ci == 0: raise _StopBuild(p, nc, out_sl, big4)
            ev_xz = [0, 0]
            t_grp = [0, 0]
            prev_conv_v = 0
            prev_z_g = [0, 0]
            for g in range(16):
                gb = (g % 2) * 4096
                for k4 in range(4):
                    p.dma("sync",
                          wbuf[:, gb + k4 * 1024:gb + (k4 + 1) * 1024].rearrange(
                              "p (a c) -> p a c", a=4),
                          r3(d_WinB, 128, 4, (k4 * 4, k4 * 4 + 4), (g * 256, g * 256 + 256)),
                          waits=[("t", t_grp[g % 2])] if g >= 2 else [])
                d_wg = p.cnt("d")
                for cc2 in range(2):
                    cc = g * 2 + cc2
                    pp = ps[cc % 2]
                    for k in range(16):
                        tm = p.emit("tensor", lambda e, pp=pp, k=k, gb=gb, cc2=cc2: e.matmul(
                            pp[:, 0:512],
                            wbuf[:, gb + k * 256 + cc2 * 128:gb + k * 256 + cc2 * 128 + 128],
                            wbuf[:, XNS + k * 512:XNS + (k + 1) * 512],
                            start=(k == 0), stop=(k == 15)),
                            waits=([("d", max(d_xn, d_wg)), ("s", ev_xz[cc % 2])]
                                   if k == 0 else []),
                            inc="t")
                    if cc2 == 1:
                        t_grp[g % 2] = tm
                    if cc < 16:
                        dk = cc
                        vhalo = p.emit("vector", lambda e, dk=dk: e.tensor_copy(
                            xpb[:, 0:2], xptail[:, dk * 2:dk * 2 + 2]),
                            waits=[("t", tm), ("v", prev_conv_v)], inc="v")
                        sxp = p.emit("scalar", lambda e, pp=pp: e.copy(xpb[:, 2:514], pp[:, 0:512]),
                                     waits=[("t", tm), ("v", prev_conv_v)], inc="s")
                        ev_xz[cc % 2] = sxp
                        p.emit("vector", lambda e, dk=dk: e.tensor_copy(
                            xptail[:, dk * 2:dk * 2 + 2], xpb[:, 512:514]),
                            waits=[("s", sxp)], inc="v")
                        i0 = 2 if ci == 0 else 1
                        p.emit("vector", lambda e, i0=i0, L=L, dk=dk: e.tensor_scalar_mul(
                            b4(B4_SP1, L), xpb[:, i0 - 1:i0 - 1 + L], cwc[:, dk * 3:dk * 3 + 1]),
                            waits=[("v", vhalo), ("s", p.cnt("s"))], inc="v")
                        p.emit("vector", lambda e, i0=i0, L=L, dk=dk: e.scalar_tensor_tensor(
                            b4(B4_SP2, L), xpb[:, i0:i0 + L], cwc[:, dk * 3 + 1:dk * 3 + 2],
                            b4(B4_SP1, L), OP.mult, OP.add), inc="v")
                        vc = p.emit("vector", lambda e, i0=i0, L=L, dk=dk: e.scalar_tensor_tensor(
                            b4(B4_SP1, L), xpb[:, i0 + 1:i0 + 1 + L], cwc[:, dk * 3 + 2:dk * 3 + 3],
                            b4(B4_SP2, L), OP.mult, OP.add), inc="v")
                        prev_conv_v = vc
                        sxc = p.emit("scalar", lambda e, L=L, dk=dk: e.activation(
                            big1[:, dk * 513:dk * 513 + L], b4(B4_SP1, L), AF.Silu,
                            bias=cbc[:, dk:dk + 1]),
                            waits=[("v", vc)], inc="s")
                        p.emit("scalar", lambda e, L=L, dk=dk: e.copy(
                            bf2[:, dk * 513:dk * 513 + L], big1[:, dk * 513:dk * 513 + L]), inc="s")
                    else:
                        dk = cc - 16
                        sz = p.emit("scalar", lambda e, pp=pp, dk=dk: e.activation(
                            b4(B4_EV[dk % 2], 512), pp[:, 0:512], AF.Silu),
                            waits=[("t", tm), ("g", prev_z_g[dk % 2])], inc="s")
                        ev_xz[cc % 2] = sz
                        prev_z_g[dk % 2] = p.dma(
                            "gpsimd", d_z[dk * 128:(dk + 1) * 128, ci * 512:(ci + 1) * 512],
                            b4(B4_EV[dk % 2], 512), waits=[("s", sz)])
            s_xcall = p.cnt("s")
            d_z_done = p.cnt("g")
            if stop_after == 31 and ci == 0:
                p.barrier(); raise _StopBuild(p, nc, out_sl, big4)
            for (pp_, c0, c1) in ((ps[2], 0, 128), (ps[3], 128, 160)):
                for k in range(16):
                    p.emit("tensor", lambda e, k=k, pp_=pp_, c0=c0, c1=c1, Lm=Lm: e.matmul(
                        pp_[0:c1 - c0, 0:Lm],
                        wxc[:, k * 160 + c0:k * 160 + c1],
                        bf2[:, k * 513:k * 513 + Lm],
                        start=(k == 0), stop=(k == 15)),
                        waits=[("s", s_xcall)] if k == 0 else [], inc="t")
            if L > 512:
                for (c0, c1, pq) in ((0, 128, ps[4]), (128, 160, ps[5])):
                    for k in range(16):
                        p.emit("tensor", lambda e, k=k, pq=pq, c0=c0, c1=c1: e.matmul(
                            pq[0:c1 - c0, 0:1],
                            wxc[:, k * 160 + c0:k * 160 + c1],
                            bf2[:, k * 513 + 512:k * 513 + 513],
                            start=(k == 0), stop=(k == 15)),
                            waits=[("s", s_xcall)] if k == 0 else [], inc="t")
            t_xd = p.cnt("t")
            p.emit("vector", lambda e, Lm=Lm: e.tensor_copy(xdl[:, 0:Lm], ps[2][:, 0:Lm]),
                   waits=[("t", t_xd)], inc="v")
            vxd = p.emit("vector", lambda e: e.tensor_copy(xdlo[0:32, 0:Lm], ps[3][0:32, 0:Lm]), inc="v")
            if L > 512:
                p.emit("vector", lambda e: e.tensor_copy(xdl[:, 512:513], ps[4][:, 0:1]), inc="v")
                vxd = p.emit("vector", lambda e: e.tensor_copy(xdlo[0:32, 512:513], ps[5][0:32, 0:1]), inc="v")
            p.dma("gpsimd", cc_xd_i[0:128, 0:L], xdl[:, 0:L], waits=[("v", vxd)])
            p.dma("gpsimd", cc_xd_i[128:160, 0:L], xdlo[0:32, 0:L])
            p.emit("gpsimd", lambda e: e.collective_compute(
                "AllReduce", OP.add, replica_groups=GROUPS,
                ins=[cc_xd_i.opt()], outs=[cc_xd_o.opt()]),
                waits=[("g", p.cnt("g"))], inc="c")
            cxd = p.cnt("c")
            p.dma("gpsimd", xdl[:, 0:L], cc_xd_o[0:128, 0:L], waits=[("c", cxd)])
            gxd = p.cnt("g")
            if stop_after == 32 and ci == 0:
                p.barrier(); raise _StopBuild(p, nc, out_sl, big4)
            sbc = p.emit("scalar", lambda e, L=L: e.copy(dtib[:, 0:L], xdl[:, 0:L]),
                         waits=[("g", gxd)], inc="s")
            if stop_after == 318 and ci == 0:
                p.barrier(); raise _StopBuild(p, nc, out_sl, big4)
            ev_dt_v = [0, 0]; ev_dt_s = [0, 0]
            SPA = (B4_SP1, B4_EV[0]); SPB = (B4_SP2, B4_EV[1])
            for dk in range(16):
                pp = ps[2 + (dk % 2)]
                sp1 = SPA[dk % 2]; sp2 = SPB[dk % 2]
                for (n0, ln) in ([(0, Lm)] + ([(512, 1)] if L > 512 else [])):
                    pq = pp if n0 == 0 else ps[4 + (dk % 2)]
                    tdt = p.emit("tensor", lambda e, dk=dk, n0=n0, ln=ln, pq=pq: e.matmul(
                        pq[:, 0:ln], wdtc[:, dk * 128:(dk + 1) * 128], dtib[:, n0:n0 + ln],
                        start=True, stop=True),
                        waits=[("s", sbc), ("v", ev_dt_v[dk % 2]), ("s", ev_dt_s[dk % 2])],
                        inc="t")
                if stop_after == 319 and ci == 0 and dk == 0:
                    p.barrier(); raise _StopBuild(p, nc, out_sl, big4)
                for (n0, ln, pq) in ([(0, Lm, pp)] + ([(512, 1, ps[4 + (dk % 2)])] if L > 512 else [])):
                    o_ = 512 if n0 else 0
                    vu = p.emit("vector", lambda e, pq=pq, dk=dk, ln=ln, sp2=sp2, o_=o_: e.tensor_scalar(
                        b4(sp2 + o_, ln), pq[:, 0:ln], bdtc[:, dk:dk + 1], None, OP.add),
                        waits=[("t", tdt), ("s", ev_dt_s[dk % 2])], inc="v")
                    ev_dt_v[dk % 2] = vu
                    p.emit("scalar", lambda e, ln=ln, sp1=sp1, sp2=sp2, o_=o_: e.activation(
                        b4(sp1 + o_, ln), b4(sp2 + o_, ln), AF.Abs),
                        waits=[("v", vu)], inc="s")
                    p.emit("scalar", lambda e, ln=ln, sp1=sp1, o_=o_: e.activation(
                        b4(sp1 + o_, ln), b4(sp1 + o_, ln), AF.Exp, scale=-1.0), inc="s")
                    ssp = p.emit("scalar", lambda e, ln=ln, sp1=sp1, o_=o_: e.activation(
                        b4(sp1 + o_, ln), b4(sp1 + o_, ln), AF.Ln, bias=1.0), inc="s")
                    ev_dt_s[dk % 2] = ssp
                    p.emit("vector", lambda e, dk=dk, n0=n0, ln=ln, sp1=sp1, sp2=sp2, o_=o_: e.scalar_tensor_tensor(
                        big2[:, dk * 513 + n0:dk * 513 + n0 + ln], b4(sp2 + o_, ln), 0.0,
                        b4(sp1 + o_, ln), OP.max, OP.add),
                        waits=[("s", ssp)], inc="v")
                if stop_after == 320 + dk and ci == 0:
                    p.barrier(); raise _StopBuild(p, nc, out_sl, big4)
            vdt = p.cnt("v")
            vdu = p.emit("vector", lambda e: e.tensor_mul(
                big3[:, 0:ND * 513], big2[:, 0:ND * 513], big1[:, 0:ND * 513]),
                waits=[("v", vdt)], inc="v")
            if stop_after == 33 and ci == 0:
                p.barrier(); raise _StopBuild(p, nc, out_sl, big4)
            ev_bc = [0, 0]
            bc_g = [0, 0]
            for j, (rb, off) in enumerate(
                    [(s_i + 16 * h, 8208 * h) for s_i in range(16) for h in (0, 1)]):
                s_i = rb % 16
                slot = B4_EV[j % 2]
                bc_g[j % 2] = p.dma(
                    "gpsimd", b4(slot, L),
                    cc_xd_o[128 + rb:129 + rb, 0:L].partition_broadcast(128),
                    waits=[("c", cxd), ("s", ev_bc[j % 2]), ("v", vdu)])
                ev_bc[j % 2] = p.emit(
                    "scalar", lambda e, off=off, s_i=s_i, slot=slot, L=L: e.copy(
                        wbuf[:, off + s_i * 513:off + s_i * 513 + L], b4(slot, L)),
                    waits=[("g", bc_g[j % 2])], inc="s")
            s_bcrep = p.cnt("s")
            if stop_after == 34 and ci == 0:
                p.barrier(); raise _StopBuild(p, nc, out_sl, big4)
            if L > 512:
                p.emit("vector", lambda e: e.memset(ytail[:], 0.0), inc="v")
            scan_hist = []
            hc_hist = []
            y1_hist = [0, 0]
            yz_g = [0, 0]
            ZW = (B4_Z, 5643)
            for dk in range(16):
                pyc = ps[6 + (dk % 2)]
                for s_i in range(16):
                    n = dk * 16 + s_i
                    par = n % 2
                    idx = dk * 16 + s_i
                    wl = [("v", scan_hist[n - 2])] if n >= 2 else [("v", vdu)]
                    sda = p.emit("scalar", lambda e, dk=dk, idx=idx, par=par, L=L: e.activation(
                        b4(B4_DA[par], L), big2[:, dk * 513:dk * 513 + L], AF.Exp,
                        scale=aneg[:, idx:idx + 1]),
                        waits=wl, inc="s")
                    p.emit("vector", lambda e, dk=dk, s_i=s_i, L=L: e.tensor_tensor(
                        b4(B4_X, L), big3[:, dk * 513:dk * 513 + L],
                        wbuf[:, s_i * 513:s_i * 513 + L], OP.mult),
                        waits=[("s", s_bcrep)] if n == 0 else [], inc="v")
                    vs = p.emit("vector", lambda e, idx=idx, par=par, L=L: e.tensor_tensor_scan(
                        b4(B4_H, L), b4(B4_DA[par], L), b4(B4_X, L),
                        hstate[:, idx:idx + 1], OP.mult, OP.add),
                        waits=[("s", sda)], inc="v")
                    scan_hist.append(vs)
                    vhc = p.emit("vector", lambda e, s_i=s_i, L=L, par=par: e.tensor_tensor(
                        hcb[:, par * 513:par * 513 + L], b4(B4_H, L),
                        wbuf[:, 8208 + s_i * 513:8208 + s_i * 513 + L], OP.mult),
                        waits=[("t", hc_hist[n - 2])] if n >= 2 else [], inc="v")
                    vsave = p.emit("vector", lambda e, idx=idx, L=L: e.tensor_copy(
                        hstate[:, idx:idx + 1], b4(B4_H + L - 1, 1)), inc="v")
                    ty = p.emit("tensor", lambda e, pyc=pyc, s_i=s_i, par=par, Lm=Lm: e.matmul(
                        pyc[:, 0:Lm], idb[:], hcb[:, par * 513:par * 513 + Lm],
                        start=(s_i == 0), stop=(s_i == 15)),
                        waits=[("v", vhc)] + ([("v", y1_hist[dk % 2])] if s_i == 0 else []),
                        inc="t")
                    hc_hist.append(ty)
                    if L > 512:
                        p.emit("vector", lambda e, dk=dk, s_i=s_i, par=par: e.scalar_tensor_tensor(
                            ytail[:, (s_i % 2) * ND + dk:(s_i % 2) * ND + dk + 1],
                            hcb[:, par * 513 + 512:par * 513 + 513], 1.0,
                            ytail[:, ((s_i + 1) % 2) * ND + dk:((s_i + 1) % 2) * ND + dk + 1],
                            OP.mult, OP.add),
                            waits=[("v", vsave)], inc="v")
                zw = ZW[dk % 2]
                gz = p.dma("gpsimd", b4(zw, L), d_z[dk * 128:(dk + 1) * 128, w0:w0 + L],
                           waits=[("g", d_z_done), ("v", y1_hist[dk % 2])])
                p.emit("vector", lambda e, dk=dk, Lm=Lm, pyc=pyc: e.scalar_tensor_tensor(
                    b4(B4_Y1, Lm), big1[:, dk * 513:dk * 513 + Lm],
                    dcol[:, dk:dk + 1], pyc[:, 0:Lm], OP.mult, OP.add),
                    waits=[("t", hc_hist[-1])], inc="v")
                if L > 512:
                    p.emit("vector", lambda e, dk=dk: e.scalar_tensor_tensor(
                        b4(B4_Y1 + 512, 1), big1[:, dk * 513 + 512:dk * 513 + 513],
                        dcol[:, dk:dk + 1], ytail[:, ND + dk:ND + dk + 1], OP.mult, OP.add),
                        waits=[("v", p.cnt("v") - 1)], inc="v")
                vyz = p.emit("vector", lambda e, L=L, dk=dk, zw=zw: e.tensor_tensor(
                    bf2[:, (dk % 2) * 513:(dk % 2) * 513 + L], b4(B4_Y1, L), b4(zw, L), OP.mult),
                    waits=[("g", gz), ("g", yz_g[dk % 2])], inc="v")
                y1_hist[dk % 2] = vyz
                yz_g[dk % 2] = p.dma(
                    "gpsimd", d_yzB[dk * 128:(dk + 1) * 128, w0:w0 + L],
                    bf2[:, (dk % 2) * 513:(dk % 2) * 513 + L], waits=[("v", vyz)])
            p.barrier()
            if stop_after == 36 and ci == 0: raise _StopBuild(p, nc, out_sl, big4)

        if stop_after < 4: raise _StopBuild(p, nc, out_sl, big4)
        ev3 = [0, 0]
        sp3 = [0, 0]
        lh_d = [0, 0]
        t_ni = 0
        for ni in range(4):
            for k4 in range(4):
                p.dma("sync", wbuf[:, k4 * 2048:(k4 + 1) * 2048].rearrange("p (a c) -> p a c", a=4),
                      r3(d_WoutB, 128, 4, (k4 * 4, k4 * 4 + 4), (ni * 512, ni * 512 + 512)),
                      waits=[("t", t_ni)] if ni > 0 else [])
            dwn = p.cnt("d")
            for m in range(16):
                pp = ps[m % 2]
                p.dma("sync", bf2[:, (m % 2) * 2048:(m % 2) * 2048 + 2048].rearrange(
                    "p (a c) -> p a c", a=16),
                    r3(d_yzB, 128, 16, (0, 16), (m * 128, m * 128 + 128)),
                    waits=[("t", lh_d[m % 2])])
                dlh = p.cnt("d")
                for k in range(16):
                    tmm = p.emit("tensor", lambda e, pp=pp, k=k, m=m: e.matmul(
                        pp[:, 0:512], bf2[:, (m % 2) * 2048 + k * 128:(m % 2) * 2048 + (k + 1) * 128],
                        wbuf[:, k * 512:(k + 1) * 512], start=(k == 0), stop=(k == 15)),
                        waits=([("d", max(dwn, dlh)), ("v", ev3[m % 2])] if k == 0 else []),
                        inc="t")
                lh_d[m % 2] = tmm
                t_ni = tmm
                vv = p.emit("vector", lambda e, pp=pp, m=m: e.tensor_copy(
                    big1[:, (m % 2) * 512:(m % 2) * 512 + 512], pp[:, 0:512]),
                    waits=[("t", tmm), ("g", sp3[m % 2])], inc="v")
                ev3[m % 2] = vv
                sp3[m % 2] = p.dma(
                    "gpsimd", rs1_i[ni // 2, m * 128:(m + 1) * 128, (ni % 2) * 512:(ni % 2) * 512 + 512],
                    big1[:, (m % 2) * 512:(m % 2) * 512 + 512], waits=[("v", vv)])
        p.emit("gpsimd", lambda e: e.collective_compute(
            "ReduceScatter", OP.add, replica_groups=GROUPS,
            ins=[rs1_i.opt()], outs=[rs1_o.opt()]),
            waits=[("g", p.cnt("g"))], inc="c")
        p.barrier()

        if stop_after < 5: raise _StopBuild(p, nc, out_sl, big4)
        st2_sum = big1[:, 7000:7016]; st2_ssq = big1[:, 7016:7032]
        st2_mx = big1[:, 7032:7048]; st2_mnn = big1[:, 7048:7064]
        p.dma("sync", b1rep[:], b1s.partition_broadcast(128))
        d_b1 = p.cnt("d")
        gh_g = {}
        ev4 = [0, 0]
        last_sgl = 0
        for sb_i in range(2):
            for k4 in range(4):
                p.dma("sync", wbuf[:, k4 * 4096:(k4 + 1) * 4096].rearrange("p (a c) -> p a c", a=4),
                      r3(d_k1T, 128, 4, (k4 * 4, k4 * 4 + 4), (sb_i * 1024, sb_i * 1024 + 1024)),
                      waits=[("t", p.cnt("t"))] if sb_i else [])
            dk1 = p.cnt("d")
            for ni in range(8):
                for k4 in range(4):
                    p.dma("sync", bf2[:, k4 * 2048:(k4 + 1) * 2048].rearrange("p (a c) -> p a c", a=4),
                          r3(d_w1qT, 128, 4, (k4 * 4, k4 * 4 + 4), (ni * 512, ni * 512 + 512)),
                          waits=[("t", p.cnt("t"))])
                drh = p.cnt("d")
                stat_src = (st_sum, st_ssq, st_mx, st_mnn) if (sb_i * 8 + ni) % 2 == 0 \
                    else (st2_sum, st2_ssq, st2_mx, st2_mnn)
                stat_dst = (st2_sum, st2_ssq, st2_mx, st2_mnn) if (sb_i * 8 + ni) % 2 == 0 \
                    else (st_sum, st_ssq, st_mx, st_mnn)
                for m in range(8):
                    M = sb_i * 8 + m
                    pp = ps[m % 2]
                    for k in range(16):
                        tmm = p.emit("tensor", lambda e, pp=pp, k=k, m=m: e.matmul(
                            pp[:, 0:512], wbuf[:, k * 1024 + m * 128:k * 1024 + m * 128 + 128],
                            bf2[:, k * 512:(k + 1) * 512], start=(k == 0), stop=(k == 15)),
                            waits=([("d", max(dk1, drh, d_b1)), ("v", ev4[m % 2])] if k == 0 else []),
                            inc="t")
                    vst = p.emit("vector", lambda e, pp=pp, M=M, ni=ni: e.scalar_tensor_tensor(
                        b4(0, 512), pp[:, 0:512], scl1[:, M:M + 1],
                        b1rep[:, ni * 512:(ni + 1) * 512], OP.mult, OP.add),
                        waits=[("t", tmm), ("s", last_sgl), ("g", gh_g.get((ni - 1, m), 0))], inc="v")
                    ev4[m % 2] = vst
                    sgl = p.emit("scalar", lambda e, m=m: e.activation(
                        big3[:, m * 513:m * 513 + 512], b4(0, 512), AF.Gelu,
                        accum_out=colA[:, m:m + 1]),
                        waits=[("v", vst)], inc="s")
                    last_sgl = sgl
                    ssq4 = p.emit("scalar", lambda e, m=m: e.activation(
                        b4(513, 512), big3[:, m * 513:m * 513 + 512], AF.Square,
                        accum_out=colA[:, 8 + m:9 + m]), inc="s")
                    p.emit("vector", lambda e, m=m, M=M, a=stat_src[0], b_=stat_dst[0]: e.tensor_tensor(
                        b_[:, M - sb_i * 8 - m + M:M - sb_i * 8 - m + M + 1] if False else b_[:, (M % 16):(M % 16) + 1],
                        a[:, (M % 16):(M % 16) + 1], colA[:, m:m + 1], OP.add),
                        waits=[("s", ssq4)], inc="v")
                    p.emit("vector", lambda e, m=m, M=M, a=stat_src[1], b_=stat_dst[1]: e.tensor_tensor(
                        b_[:, (M % 16):(M % 16) + 1], a[:, (M % 16):(M % 16) + 1],
                        colA[:, 8 + m:9 + m], OP.add), inc="v")
                    p.emit("vector", lambda e, m=m: e.tensor_reduce(
                        colA[:, 16 + m:17 + m], big3[:, m * 513:m * 513 + 512], AX, OP.max), inc="v")
                    vminr = p.emit("vector", lambda e, m=m: e.tensor_reduce(
                        colA[:, 24 + m:25 + m], big3[:, m * 513:m * 513 + 512], AX, OP.min), inc="v")
                    p.emit("vector", lambda e, m=m, M=M, a=stat_src[2], b_=stat_dst[2]: e.tensor_max(
                        b_[:, (M % 16):(M % 16) + 1], a[:, (M % 16):(M % 16) + 1],
                        colA[:, 16 + m:17 + m]), inc="v")
                    vmn = p.emit("vector", lambda e, m=m, M=M, a=stat_src[3], b_=stat_dst[3]: e.scalar_tensor_tensor(
                        b_[:, (M % 16):(M % 16) + 1], colA[:, 24 + m:25 + m], -1.0,
                        a[:, (M % 16):(M % 16) + 1], OP.mult, OP.max),
                        waits=[("v", vminr)], inc="v")
                    gh_g[(ni, m)] = p.dma(
                        "gpsimd", d_gh[M * 128:(M + 1) * 128, ni * 512:(ni + 1) * 512],
                        big3[:, m * 513:m * 513 + 512], waits=[("v", vmn)])
        vstat = p.cnt("v")
        p.dma("gpsimd", cc_st_i[:, 0:16], st_sum[:], waits=[("v", vstat)])
        p.dma("gpsimd", cc_st_i[:, 16:32], st_ssq[:])
        p.dma("gpsimd", cc_sm_i[:, 0:16], st_mx[:])
        p.dma("gpsimd", cc_sm_i[:, 16:32], st_mnn[:])
        p.emit("gpsimd", lambda e: e.collective_compute(
            "AllReduce", OP.add, replica_groups=GROUPS,
            ins=[cc_st_i.opt()], outs=[cc_st_o.opt()]),
            waits=[("g", p.cnt("g"))], inc="c")
        p.emit("gpsimd", lambda e: e.collective_compute(
            "AllReduce", OP.max, replica_groups=GROUPS,
            ins=[cc_sm_i.opt()], outs=[cc_sm_o.opt()]),
            inc="c")
        cst = p.cnt("c")
        p.dma("gpsimd", st_sum[:], cc_st_o[:, 0:16], waits=[("c", cst)])
        p.dma("gpsimd", st_ssq[:], cc_st_o[:, 16:32])
        p.dma("gpsimd", st_mx[:], cc_sm_o[:, 0:16])
        p.dma("gpsimd", st_mnn[:], cc_sm_o[:, 16:32])
        gstat = p.cnt("g")
        vq0 = p.emit("vector", lambda e: e.tensor_scalar_mul(nm2c[:], st_sum[:], -1.0 / 8192.0),
                     waits=[("g", gstat)], inc="v")
        vq1 = p.emit("vector", lambda e: e.tensor_mul(colA[:, 0:16], nm2c[:], nm2c[:]),
                     waits=[("v", vq0)], inc="v")
        vq2a = p.emit("vector", lambda e: e.tensor_scalar_mul(colB[:, 0:16], st_ssq[:], 1.0 / 8192.0), inc="v")
        vv2b = p.emit("vector", lambda e: e.tensor_sub(colB[:, 0:16], colB[:, 0:16], colA[:, 0:16]),
                      waits=[("v", vq2a)], inc="v")
        p.emit("scalar", lambda e: e.activation(colA[:, 16:32], colB[:, 0:16], AF.Ln, bias=kcol[:, 0:1]),
               waits=[("v", vv2b)], inc="s")
        sr2b = p.emit("scalar", lambda e: e.activation(colB[:, 16:32], colA[:, 16:32], AF.Exp, scale=kcol[:, 1:2]), inc="s")
        vb1 = p.emit("vector", lambda e: e.tensor_add(colA[:, 0:16], st_mx[:], nm2c[:]),
                     waits=[("s", sr2b)], inc="v")
        vb2 = p.emit("vector", lambda e: e.tensor_sub(colB[:, 0:16], st_mnn[:], nm2c[:]), inc="v")
        vb3 = p.emit("vector", lambda e: e.tensor_max(colA[:, 0:16], colA[:, 0:16], colB[:, 0:16]),
                     waits=[("v", vb2)], inc="v")
        vb4 = p.emit("vector", lambda e: e.tensor_mul(colA[:, 0:16], colA[:, 0:16], colB[:, 16:32]),
                     waits=[("v", vb3)], inc="v")
        vb5 = p.emit("vector", lambda e: e.tensor_scalar_max(colB[:, 0:16], colA[:, 0:16], EPS),
                     waits=[("v", vb4)], inc="v")
        vb6 = p.emit("vector", lambda e: e.reciprocal(q2c[:], colB[:, 0:16]),
                     waits=[("v", vb5)], inc="v")
        vb7 = p.emit("vector", lambda e: e.tensor_scalar_mul(q2c[:], q2c[:], 127.0),
                     waits=[("v", vb6)], inc="v")
        vb8 = p.emit("vector", lambda e: e.tensor_mul(q2c[:], q2c[:], colB[:, 16:32]),
                     waits=[("v", vb7)], inc="v")
        p.emit("vector", lambda e: e.tensor_scalar(scl2[:], colB[:, 0:16], thr[:, 5:6], 1.0 / 127.0,
                                                   OP.mult, OP.mult), inc="v")
        vq2 = p.emit("vector", lambda e: e.tensor_mul(nm2c[:], nm2c[:], q2c[:]),
                     waits=[("v", vb8)], inc="v")
        p.barrier()

        if stop_after < 6: raise _StopBuild(p, nc, out_sl, big4)
        gh_r = [0, 0]
        for M in range(16):
            buf = big3[:, (M % 2) * 4104:(M % 2) * 4104 + 4096]
            p.dma("sync", buf, d_gh[M * 128:(M + 1) * 128, :], waits=[("v", gh_r[M % 2])])
            dgh = p.cnt("d")
            p.emit("vector", lambda e, b=buf, M=M: e.tensor_scalar(
                big4[:, 0:4096], b, q2c[:, M:M + 1], nm2c[:, M:M + 1], OP.mult, OP.add),
                waits=[("d", dgh)], inc="v")
            vk2 = p.emit("vector", lambda e: e.tensor_scalar(
                big4[:, 4104:8200], big4[:, 0:4096], MAGIC, -MAGIC, OP.add, OP.add),
                waits=[("t", p.cnt("t"))], inc="v")
            gh_r[M % 2] = vk2
            for jb in range(32):
                pt = ps[4 + (jb % 2)]
                tw = p.emit("tensor", lambda e, pt=pt, jb=jb: e.transpose(
                    pt[:, 0:128], big4[:, 4104 + jb * 128:4104 + (jb + 1) * 128], idf[:]),
                    waits=[("v", vk2), ("s", ev_s[jb % 2])], inc="t")
                ev_s[jb % 2] = p.emit("scalar", lambda e, pt=pt, jb=jb: e.copy(
                    bf2[:, 4104 + (jb % 2) * 128:4232 + (jb % 2) * 128], pt[:, 0:128]),
                    waits=[("t", tw), ("d", ev_d[jb % 2])], inc="s")
                ev_d[jb % 2] = p.dma(
                    "sync", d_k2T[jb * 128:(jb + 1) * 128, M * 128:(M + 1) * 128],
                    bf2[:, 4104 + (jb % 2) * 128:4232 + (jb % 2) * 128],
                    waits=[("s", ev_s[jb % 2])])
        p.barrier()

        if stop_after < 7: raise _StopBuild(p, nc, out_sl, big4)
        ev5 = [0, 0]
        sp5 = [0, 0]
        k2_d = [0, 0]
        t_ni = 0
        for ni in range(4):
            for k4 in range(8):
                p.dma("sync", wbuf[:, k4 * 2048:(k4 + 1) * 2048].rearrange("p (a c) -> p a c", a=4),
                      r3(d_w2qT, 128, 4, (k4 * 4, k4 * 4 + 4), (ni * 512, ni * 512 + 512)),
                      waits=[("t", t_ni)] if ni > 0 else [])
            dwn = p.cnt("d")
            for m in range(16):
                pp = ps[m % 2]
                p.dma("sync", bf2[:, (m % 2) * 4104:(m % 2) * 4104 + 4096].rearrange(
                    "p (a c) -> p a c", a=32),
                    r3(d_k2T, 128, 32, (0, 32), (m * 128, m * 128 + 128)),
                    waits=[("t", k2_d[m % 2])])
                dlh = p.cnt("d")
                for k in range(32):
                    tmm = p.emit("tensor", lambda e, pp=pp, k=k, m=m: e.matmul(
                        pp[:, 0:512], bf2[:, (m % 2) * 4104 + k * 128:(m % 2) * 4104 + (k + 1) * 128],
                        wbuf[:, k * 512:(k + 1) * 512], start=(k == 0), stop=(k == 31)),
                        waits=([("d", max(dwn, dlh)), ("v", ev5[m % 2])] if k == 0 else []),
                        inc="t")
                k2_d[m % 2] = tmm
                t_ni = tmm
                vv = p.emit("vector", lambda e, pp=pp, m=m: e.tensor_copy(
                    big1[:, (m % 2) * 512:(m % 2) * 512 + 512], pp[:, 0:512]),
                    waits=[("t", tmm), ("g", sp5[m % 2])], inc="v")
                ev5[m % 2] = vv
                sp5[m % 2] = p.dma(
                    "gpsimd", rs2_i[ni // 2, m * 128:(m + 1) * 128, (ni % 2) * 512:(ni % 2) * 512 + 512],
                    big1[:, (m % 2) * 512:(m % 2) * 512 + 512], waits=[("v", vv)])
        p.emit("gpsimd", lambda e: e.collective_compute(
            "ReduceScatter", OP.add, replica_groups=GROUPS,
            ins=[rs2_i.opt()], outs=[rs2_o.opt()]),
            waits=[("g", p.cnt("g"))], inc="c")
        p.barrier()

        if stop_after < 8: raise _StopBuild(p, nc, out_sl, big4)
        p.dma("sync", b2rep[:], b2s.partition_broadcast(128))
        db2 = p.cnt("d")
        out_d = [0, 0]
        out_dma = [0, 0]
        for M in range(16):
            o1 = (M % 2) * 4104
            p.dma("sync", big3[:, o1:o1 + 1024], rs1_o[M * 128:(M + 1) * 128, :],
                  waits=[("v", out_d[M % 2])])
            p.dma("sync", big3[:, o1 + 1026:o1 + 2050], rs2_o[M * 128:(M + 1) * 128, :])
            p.dma("sync", big3[:, o1 + 2052:o1 + 3076], x_res[M * 128:(M + 1) * 128, :])
            dw = p.cnt("d")
            vg = p.emit("vector", lambda e, o1=o1, M=M: e.scalar_tensor_tensor(
                big4[:, (M % 2) * 2048:(M % 2) * 2048 + 1024], big3[:, o1 + 1026:o1 + 2050],
                scl2[:, M:M + 1], b2rep[:], OP.mult, OP.add),
                waits=[("d", max(dw, db2, out_dma[M % 2]))], inc="v")
            sg_ = p.emit("scalar", lambda e, M=M: e.activation(
                big4[:, (M % 2) * 2048 + 1024:(M % 2) * 2048 + 2048],
                big4[:, (M % 2) * 2048:(M % 2) * 2048 + 1024], AF.Silu),
                waits=[("v", vg)], inc="s")
            p.emit("vector", lambda e, o1=o1, M=M: e.tensor_tensor(
                big4[:, (M % 2) * 2048:(M % 2) * 2048 + 1024],
                big4[:, (M % 2) * 2048 + 1024:(M % 2) * 2048 + 2048],
                big3[:, o1:o1 + 1024], OP.mult),
                waits=[("s", sg_)], inc="v")
            vout = p.emit("vector", lambda e, o1=o1, M=M: e.tensor_tensor(
                big4[:, (M % 2) * 2048:(M % 2) * 2048 + 1024],
                big4[:, (M % 2) * 2048:(M % 2) * 2048 + 1024],
                big3[:, o1 + 2052:o1 + 3076], OP.add), inc="v")
            out_d[M % 2] = vout
            out_dma[M % 2] = p.dma("sync", out_sl[M * 128:(M + 1) * 128, :],
                                   big4[:, (M % 2) * 2048:(M % 2) * 2048 + 1024],
                                   waits=[("v", vout)])
        for nm in DUMP:
            ap_, shp, dt_ = p.dbg[nm]
            dbg_o = nc.dram_tensor("dbg_" + nm, shp, dt_, kind="ExternalOutput").ap()
            p.dma("sync", dbg_o, ap_)
        dfin = p.cnt("d")
        p.q["sync"].append(([("d", dfin)], None, None, 0))

        _finish(nc, p)
    return nc


def _finish(nc, p):
        with nc.Block() as block:
            @block.sync
            def _(eng):
                p.run("sync", eng)

            @block.tensor
            def _(eng):
                p.run("tensor", eng)

            @block.vector
            def _(eng):
                p.run("vector", eng)

            @block.scalar
            def _(eng):
                p.run("scalar", eng)

            @block.gpsimd
            def _(eng):
                p.run("gpsimd", eng)


_NC_CACHE = None
_LAST_RES = None


def kernel(L0_attn_x, L0_attn_W_in, L0_attn_conv_w, L0_attn_conv_b, L0_attn_W_x,
           L0_attn_W_dt, L0_attn_b_dt, L0_attn_A_log, L0_attn_D, L0_attn_W_out,
           L0_ffn_w1, L0_ffn_b1, L0_ffn_w2, L0_ffn_b2):
    global _NC_CACHE
    f32 = np.float32
    bf16 = ml_dtypes.bfloat16

    def col16(v):
        return np.ascontiguousarray(np.asarray(v, f32).reshape(16, 128).T)

    x = np.asarray(L0_attn_x, f32)
    W_in = np.asarray(L0_attn_W_in, f32)
    conv_w = np.asarray(L0_attn_conv_w, f32)
    conv_b = np.asarray(L0_attn_conv_b, f32)
    W_x = np.asarray(L0_attn_W_x, f32)
    W_dt = np.asarray(L0_attn_W_dt, f32)
    b_dt = np.asarray(L0_attn_b_dt, f32)
    A_log = np.asarray(L0_attn_A_log, f32)
    D = np.asarray(L0_attn_D, f32)
    W_out = np.asarray(L0_attn_W_out, f32)
    w1 = np.asarray(L0_ffn_w1, f32)
    b1 = np.asarray(L0_ffn_b1, f32)
    w2 = np.asarray(L0_ffn_w2, f32)
    b2 = np.asarray(L0_ffn_b2, f32)

    ident = np.eye(128, dtype=bf16)
    ones_r = np.ones((1, 128), f32)
    ones_rb = np.ones((1, 128), bf16)
    ones_c = np.ones((128, 1), f32)

    in_maps = []
    for c in range(8):
        b, hf = c // 2, c % 2
        dlo, dhi = hf * 2048, (hf + 1) * 2048
        flo, fhi = hf * 4096, (hf + 1) * 4096
        olo, ohi = hf * 1024, (hf + 1) * 1024
        im = {
            "x_b": np.ascontiguousarray(x[b]),
            "x_res": np.ascontiguousarray(x[b][:, olo:ohi]),
            "Win": np.ascontiguousarray(
                np.concatenate([W_in[:, dlo:dhi], W_in[:, 4096 + dlo:4096 + dhi]], axis=1)),
            "convw": np.ascontiguousarray(
                conv_w[dlo:dhi, 0, :].reshape(16, 128, 3).transpose(1, 0, 2).reshape(128, 48)),
            "convb": col16(conv_b[dlo:dhi]),
            "Wx": np.ascontiguousarray(
                W_x[dlo:dhi].reshape(16, 128, 160).transpose(1, 0, 2).reshape(128, 16 * 160)),
            "Wdt": np.ascontiguousarray(W_dt[:, dlo:dhi]),
            "bdt": col16(b_dt[dlo:dhi]),
            "Alog": np.ascontiguousarray(
                A_log[dlo:dhi].reshape(16, 128, 16).transpose(1, 0, 2).reshape(128, 256)),
            "Dsl": col16(D[dlo:dhi]),
            "Wout": np.ascontiguousarray(W_out[dlo:dhi]),
            "w1s": np.ascontiguousarray(w1[flo:fhi]),
            "w2s": np.ascontiguousarray(w2[:, flo:fhi]),
            "b1s": np.ascontiguousarray(b1[flo:fhi][None, :]),
            "b2s": np.ascontiguousarray(b2[olo:ohi][None, :]),
            "ident": ident, "identf": np.eye(128, dtype=f32), "ones_r": ones_r, "ones_rb": ones_rb, "ones_c": ones_c,
        }
        in_maps.append(im)

    if _NC_CACHE is None:
        _NC_CACHE = build_program()
    res = run_bass_kernel_spmd(_NC_CACHE, in_maps, core_ids=list(range(8)))
    global _LAST_RES
    _LAST_RES = res
    out = np.empty((4, 2048, 2048), np.float32)
    for c in range(8):
        b, hf = c // 2, c % 2
        out[b][:, hf * 1024:(hf + 1) * 1024] = np.asarray(res.results[c]["out_sl"])
    return out
